# version 09
# baseline (speedup 1.0000x reference)
import numpy as np
from contextlib import ExitStack

import concourse.bass as bass
import concourse.mybir as mybir
from concourse.bass_utils import run_bass_kernel_spmd

import concourse.bass_utils as _bu

if not getattr(_bu, "_ldw_opt_patched", False):
    _orig_run_command = _bu.run_command

    def _run_command_ldw(cmd, **kw):
        if isinstance(cmd, list):
            cmd = ["--enable-ldw-opt=true" if c == "--enable-ldw-opt=false" else c
                   for c in cmd]
        return _orig_run_command(cmd, **kw)

    _bu.run_command = _run_command_ldw
    _bu._ldw_opt_patched = True


F32 = mybir.dt.float32
BF16 = mybir.dt.bfloat16
I32 = mybir.dt.int32
AF = mybir.ActivationFunctionType
ALU = mybir.AluOpType
AX = mybir.AxisListType.X

D = 1024
I = 2048
DFF = 4096
NE = 8
NCORES = 8
BPC = 4
SEQ = 2048
TOK = BPC * SEQ
PT = 512
TCH = PT // 128
KD = D // 128
KI = I // 128
KF = DFF // 128
NEWTON_C = 0x5F3759DF


class Sem:
    def __init__(self, h):
        self.h = h
        self.n = 0

    def bump(self, k=1):
        self.n += k
        return self.n


def build_nc(n_tiles=16):
    nc = bass.Bass("TRN2", target_bir_lowering=False, debug=False)

    d_x = nc.dram_tensor("x", [TOK, D], F32, kind="ExternalInput").ap()
    d_ip = nc.dram_tensor("w_ip", [D, 2 * I], F32, kind="ExternalInput").ap()
    d_conv = nc.dram_tensor("w_conv", [4, 128, 4096], F32, kind="ExternalInput").ap()
    d_fc = nc.dram_tensor("w_fc", [I, D], F32, kind="ExternalInput").ap()
    d_cp = nc.dram_tensor("w_cproj", [D, I], F32, kind="ExternalInput").ap()
    d_out = nc.dram_tensor("w_out", [I, D], F32, kind="ExternalInput").ap()
    d_rt = nc.dram_tensor("w_router", [D, NE], F32, kind="ExternalInput").ap()
    d_efc = nc.dram_tensor("w_efc", [NE, D, DFF], F32, kind="ExternalInput").ap()
    d_epj = nc.dram_tensor("w_eproj", [NE, DFF, D], F32, kind="ExternalInput").ap()
    d_sfc = nc.dram_tensor("w_sfc", [D, DFF], F32, kind="ExternalInput").ap()
    d_spj = nc.dram_tensor("w_sproj", [DFF, D], F32, kind="ExternalInput").ap()
    d_cb = nc.dram_tensor("b_conv", [128, KI], F32, kind="ExternalInput").ap()
    d_lg = nc.dram_tensor("b_lng", [128, KI], F32, kind="ExternalInput").ap()
    d_lb = nc.dram_tensor("b_lnb", [128, KI], F32, kind="ExternalInput").ap()
    d_fb = nc.dram_tensor("b_fc", [128, KD], F32, kind="ExternalInput").ap()
    d_pb = nc.dram_tensor("b_cproj", [128, KI], F32, kind="ExternalInput").ap()
    d_id = nc.dram_tensor("ident", [128, 128], F32, kind="ExternalInput").ap()
    d_o = nc.dram_tensor("out", [TOK, D], F32, kind="ExternalOutput").ap()
    d_mir = nc.dram_tensor("wmir", [128, 688128], BF16, kind="Internal").ap()

    with ExitStack() as es:
        E = es.enter_context
        x_t = [E(nc.sbuf_tensor(f"x{p}", [128, TCH * D], F32)) for p in range(2)]
        xn = E(nc.sbuf_tensor("xn", [128, KD * PT], BF16))
        hbuf = E(nc.sbuf_tensor("hbuf", [128, 2 * 515], BF16))
        halo = E(nc.sbuf_tensor("halo", [128, KI * 3], BF16))
        hg = E(nc.sbuf_tensor("hg", [128, KI * PT], F32))
        hn_r2 = E(nc.sbuf_tensor("hn_r2", [128, KI * PT], BF16))
        s_sb = E(nc.sbuf_tensor("s_sb", [128, KD * PT], BF16))
        h2 = E(nc.sbuf_tensor("h2", [128, KI * PT], BF16))
        wslot = [E(nc.sbuf_tensor(f"ws{i}", [128, 8192], BF16)) for i in range(3)]
        xns = [E(nc.sbuf_tensor(f"xns{p}", [128, D], BF16)) for p in range(2)]
        gt = [E(nc.sbuf_tensor(f"gt{p}", [128, PT], F32)) for p in range(2)]
        ct = [E(nc.sbuf_tensor(f"ct{p}", [128, PT], F32)) for p in range(2)]
        rt_ = [E(nc.sbuf_tensor(f"rt{p}", [128, PT], F32)) for p in range(2)]
        u1 = gt
        e1 = ct
        a2sb = E(nc.sbuf_tensor("a2sb", [128, PT], F32))
        b2sb = E(nc.sbuf_tensor("b2sb", [128, PT], F32))
        rows = E(nc.sbuf_tensor("rows", [1, 8 * PT], F32))
        cols = E(nc.sbuf_tensor("cols", [128, 40], F32))
        bnt4 = E(nc.sbuf_tensor("bnt4", [128, 48], F32))
        bna4 = E(nc.sbuf_tensor("bna4", [128, 8], F32))
        sh4 = E(nc.sbuf_tensor("sh4", [128, 4], I32))
        lgt = E(nc.sbuf_tensor("lgt", [8, PT], F32))
        lgtm = E(nc.sbuf_tensor("lgtm", [128, TCH * 8], F32))
        comb = E(nc.sbuf_tensor("comb", [128, TCH * 8], F32))
        m8 = E(nc.sbuf_tensor("m8", [128, TCH * 8], F32))
        wrt = E(nc.sbuf_tensor("wrt", [128, KD * 8], BF16))
        biases = E(nc.sbuf_tensor("biases", [128, 4 * KI + KD], F32))
        id32 = E(nc.sbuf_tensor("id32", [128, 128], F32))
        id16 = E(nc.sbuf_tensor("id16", [128, 128], BF16))
        onesc = E(nc.sbuf_tensor("onesc", [128, 1], F32))
        onesr = E(nc.sbuf_tensor("onesr", [1, 128], F32))
        sh1 = E(nc.sbuf_tensor("sh1", [128, 1], I32))
        shrow = E(nc.sbuf_tensor("shrow", [1, PT], I32))

        pb = [E(nc.psum_tensor(f"pb{i}", [128, 512], F32)) for i in range(7)]
        pt16 = E(nc.psum_tensor("pt16", [128, 512], BF16))

        s_pe = Sem(E(nc.semaphore("s_pe")))
        s_ve = Sem(E(nc.semaphore("s_ve")))
        s_se = Sem(E(nc.semaphore("s_se")))
        s_gp = Sem(E(nc.semaphore("s_gp")))
        s_x = Sem(E(nc.semaphore("s_x")))
        s_st = Sem(E(nc.semaphore("s_st")))
        s_gd = Sem(E(nc.semaphore("s_gd")))
        s_wl = [Sem(E(nc.semaphore(f"s_wl{i}"))) for i in range(3)]
        s_wf = [Sem(E(nc.semaphore(f"s_wf{i}"))) for i in range(3)]

        block = E(nc.Block())

        P = {"t": [], "v": [], "s": [], "g": [], "y": []}

        def em(key, fn):
            P[key].append(fn)

        def wait(key, sem, val):
            if val > 0:
                em(key, lambda e, s=sem.h, v=val: e.wait_ge(s, v))

        guards = {}

        def pe_acquire(bk):
            g = guards.pop(bk, None)
            if g is not None:
                wait("t", g[0], g[1])

        def set_guard(bk, sem):
            guards[bk] = (sem, sem.n)

        ring_next = [0]
        ring_loads = [0, 0, 0]
        ring_uses = [0, 0, 0]
        cur = {"ti": 0, "j": 0, "off": 0}
        mir_offs = []
        mir_store_ms = []
        mir_pending = [0, 0, 0]

        def ring_load(dma_fn_list, cols):
            i = ring_next[0]
            ring_next[0] = (i + 1) % 3
            j = cur["j"]
            cur["j"] += 1
            if cur["ti"] == 0:
                mir_offs.append(cur["off"])
                cur["off"] += cols
            off = mir_offs[j]
            slot = wslot[i]
            if cur["ti"] == 0:
                if ring_uses[i] > 0:
                    wait("g", s_wf[i], ring_uses[i])
                    if mir_pending[i]:
                        wait("g", s_st, mir_pending[i])
                for fn in dma_fn_list:
                    dst, src = fn(slot)
                    ring_loads[i] += 1
                    s_wl[i].n = 16 * ring_loads[i]
                    em("g", lambda g, d=dst, s=src, sm=s_wl[i].h: g.dma_start(d, s).then_inc(sm, 16))
                tgt = 16 * ring_loads[i]
                wait("y", s_wl[i], tgt)
                s_st.n += 16
                em("y", lambda y, d=d_mir[:, off:off + cols], s=slot[:, 0:cols], sm=s_st.h:
                   y.dma_start(d, s).then_inc(sm, 16))
                mir_pending[i] = s_st.n
                mir_store_ms.append(s_st.n)
            else:
                q = "y" if (j % 2 == 0) else "g"
                if q == "g" and cur["ti"] == 1:
                    wait(q, s_st, mir_store_ms[j])
                wait(q, s_wf[i], ring_uses[i])
                ring_loads[i] += 1
                tgt = 16 * ring_loads[i]
                s_wl[i].n = tgt
                em(q, lambda e, d=slot[:, 0:cols], s=d_mir[:, off:off + cols], sm=s_wl[i].h:
                   e.dma_start(d, s).then_inc(sm, 16))
            ring_uses[i] += 1
            return (i, 16 * ring_loads[i], slot)

        def ring_wait(tok):
            wait("t", s_wl[tok[0]], tok[1])

        def ring_free_inc(tok):
            s_wf[tok[0]].bump()
            return s_wf[tok[0]].h

        def mm(out_ap, lhs_ap, rhs_ap, start, stop, inc_sem=None):
            if inc_sem is None:
                em("t", lambda t, o=out_ap, l=lhs_ap, r=rhs_ap, a=start, b=stop:
                   t.matmul(o, l, r, start=a, stop=b))
            else:
                em("t", lambda t, o=out_ap, l=lhs_ap, r=rhs_ap, a=start, b=stop, s=inc_sem:
                   t.matmul(o, l, r, start=a, stop=b).then_inc(s, 1))

        def pe_transpose(out_ap, in_ap, ident, inc_sem=None):
            if inc_sem is None:
                em("t", lambda t, o=out_ap, i=in_ap, d=ident: t.transpose(o, i, d))
            else:
                em("t", lambda t, o=out_ap, i=in_ap, d=ident, s=inc_sem:
                   t.transpose(o, i, d).then_inc(s, 1))

        def vop(name, *aps, inc=False, **kw):
            if inc:
                s_ve.bump()
                em("v", lambda v, n=name, a=aps, k=kw, s=s_ve.h: getattr(v, n)(*a, **k).then_inc(s, 1))
            else:
                em("v", lambda v, n=name, a=aps, k=kw: getattr(v, n)(*a, **k))
            return s_ve.n

        def sop(name, *aps, inc=False, **kw):
            if inc:
                s_se.bump()
                em("s", lambda s_, n=name, a=aps, k=kw, s=s_se.h: getattr(s_, n)(*a, **k).then_inc(s, 1))
            else:
                em("s", lambda s_, n=name, a=aps, k=kw: getattr(s_, n)(*a, **k))
            return s_se.n

        def gop(name, *aps, inc=False, **kw):
            if inc:
                s_gp.bump()
                em("g", lambda g, n=name, a=aps, k=kw, s=s_gp.h: getattr(g, n)(*a, **k).then_inc(s, 1))
            else:
                em("g", lambda g, n=name, a=aps, k=kw: getattr(g, n)(*a, **k))
            return s_gp.n

        def pe_inc():
            s_pe.bump()
            return s_pe.h

        def vbar():
            m = vop("tensor_copy", cols[:, 37:38], cols[:, 36:37], inc=True)
            wait("v", s_ve, m)
            return m

        v_cb = biases[:, 0:KI]
        v_lg = biases[:, KI:2 * KI]
        v_lb = biases[:, 2 * KI:3 * KI]
        v_pb = biases[:, 3 * KI:4 * KI]
        v_fb = biases[:, 4 * KI:4 * KI + KD]

        def row(i):
            return rows[0:1, i * PT:(i + 1) * PT]

        def col(i):
            return cols[:, i:i + 1]

        for dst, src in [(biases[:, 0:KI], d_cb), (biases[:, KI:2 * KI], d_lg),
                         (biases[:, 2 * KI:3 * KI], d_lb), (biases[:, 3 * KI:4 * KI], d_pb),
                         (biases[:, 4 * KI:4 * KI + KD], d_fb), (id32[:, :], d_id)]:
            em("y", lambda y, d=dst, s=src: y.dma_start(d, s).then_inc(s_x.h, 16))
            s_x.n += 16
        em("g", lambda g: g.dma_start(wrt[:, :].rearrange("p (k e) -> p k e", k=KD),
                                      d_rt.rearrange("(k p) e -> p k e", p=128)).then_inc(s_gd.h, 16))
        em("g", lambda g: g.dma_start(id16[:, :], d_id).then_inc(s_gd.h, 16))
        s_gd.n += 32
        CONST_GP = s_gd.n
        CONST_SY = s_x.n
        wait("v", s_x, CONST_SY)
        vop("memset", onesc[:, :], 1.0)
        vop("memset", onesr[:, :], 1.0)
        vop("memset", sh1[:, :], 1)
        vop("memset", sh4[:, :], 1)
        vop("memset", cols[:, :], 0.0)
        CONST_VE = vop("memset", shrow[:, :], 1, inc=True)

        x_loaded = [0] * n_tiles
        store_done = [0] * n_tiles

        xn_free_pe = 0
        hg_last_tile = 0
        hbuf_free = None
        acc_done_prev = 0

        def newton_rsqrt(dst, src, t1, t2, t3, shift_ap):
            m = vop("tensor_tensor", t1.bitcast(I32), src.bitcast(I32), shift_ap,
                    ALU.logical_shift_right, inc=True)
            wait("v", s_ve, m)
            m = vop("tensor_scalar", t1.bitcast(I32), t1.bitcast(I32), -1, NEWTON_C,
                    ALU.mult, ALU.add, inc=True)
            wait("v", s_ve, m)
            cur, nxt = t1, dst
            for _ in range(3):
                m = vop("tensor_tensor", t2, cur, cur, ALU.mult, inc=True)
                wait("v", s_ve, m)
                m = vop("tensor_tensor", t3, t2, src, ALU.mult, inc=True)
                wait("v", s_ve, m)
                m = vop("tensor_scalar", t3, t3, -0.5, 1.5, ALU.mult, ALU.add, inc=True)
                wait("v", s_ve, m)
                m = vop("tensor_tensor", nxt, cur, t3, ALU.mult, inc=True)
                wait("v", s_ve, m)
                cur, nxt = nxt, cur
            if cur is not dst:
                m = vop("tensor_copy", dst, cur, inc=True)
                wait("v", s_ve, m)

        def plan_A(ati, ams):
            axt = x_t[ati % 2]
            wait("v", s_x, x_loaded[ati])
            wait("v", s_ve, hg_last_tile)
            for tc in range(TCH):
                xv = axt[:, tc * D:(tc + 1) * D]
                vop("bn_stats", bnt4[:, 12 * tc:12 * tc + 6], xv[:, 0:PT])
                vop("bn_stats", bnt4[:, 12 * tc + 6:12 * tc + 12], xv[:, PT:2 * PT])
                yield
            vbar()
            for tc in range(TCH):
                vop("bn_aggr", bna4[:, 2 * tc:2 * tc + 2], bnt4[:, 12 * tc:12 * tc + 12])
            vbar()
            yield
            mean_v = bna4[:, :].rearrange("p (t z) -> p t z", z=2)[:, :, 0]
            var_v = bna4[:, :].rearrange("p (t z) -> p t z", z=2)[:, :, 1]
            m = vop("tensor_scalar", cols[:, 12:16], var_v, 1e-5, None, ALU.add, inc=True)
            wait("v", s_ve, m)
            yield
            newton_rsqrt(cols[:, 16:20], cols[:, 12:16], cols[:, 24:28], cols[:, 28:32],
                         cols[:, 32:36], sh4[:, :])
            yield
            m = vop("tensor_tensor", cols[:, 32:36], mean_v, cols[:, 16:20], ALU.mult, inc=True)
            wait("v", s_ve, m)
            m = vop("tensor_scalar", cols[:, 20:24], cols[:, 32:36], -1.0, None, ALU.mult, inc=True)
            wait("v", s_ve, m)
            yield
            tr_ms = [0] * TCH
            for tc in range(TCH):
                sp = tc % 2
                if tc >= 2:
                    wait("v", s_pe, tr_ms[tc - 2])
                m_st = vop("tensor_scalar", xns[sp][:, :], axt[:, tc * D:(tc + 1) * D],
                           col(16 + tc), col(20 + tc), ALU.mult, ALU.add, inc=True)
                wait("t", s_ve, m_st)
                if ati == 0 and tc == 0:
                    wait("t", s_gd, CONST_GP)
                if ati > 0 and tc == 0:
                    wait("t", s_pe, xn_free_pe)
                yield
                for half in range(2):
                    pe_acquire("pt16")
                    for q in range(4):
                        c = half * 4 + q
                        pe_transpose(pt16[:, q * 128:(q + 1) * 128],
                                     xns[sp][:, c * 128:(c + 1) * 128], id16[:, :],
                                     inc_sem=(pe_inc() if q == 3 else None))
                    wait("v", s_pe, s_pe.n)
                    xnv = xn[:, :].rearrange("p (k t) -> p k t", k=KD)[
                        :, half * 4:half * 4 + 4, tc * 128:(tc + 1) * 128]
                    vop("tensor_copy", xnv,
                        pt16[:, 0:512].rearrange("p (q c) -> p q c", q=4), inc=True)
                    set_guard("pt16", s_ve)
                    yield
                tr_ms[tc] = s_pe.n
            ams["xn_ready"] = s_ve.n

        gA_hold = {"gen": None, "ams": None}

        def drain_A():
            g = gA_hold["gen"]
            if g is not None:
                try:
                    next(g)
                except StopIteration:
                    gA_hold["gen"] = None

        def emit_xload(ti2):
            if ti2 >= 2:
                wait("y", s_st, store_done[ti2 - 2])
            a2 = x_t[ti2 % 2]
            em("y", lambda y, a=a2, src=d_x[ti2 * PT:(ti2 + 1) * PT, :]:
               y.dma_start(a[:, :].rearrange("p (t f) -> p t f", t=TCH),
                           src.rearrange("(t p) f -> p t f", p=128)).then_inc(s_x.h, 16))
            s_x.n += 16
            x_loaded[ti2] = s_x.n

        for ti in range(n_tiles):
            cur["ti"] = ti
            cur["j"] = 0
            par = ti % 2
            xt = x_t[par]

            if ti == 0:
                emit_xload(0)
                ams0 = {}
                for _ in plan_A(0, ams0):
                    pass
                xn_ready = ams0["xn_ready"]
            else:
                xn_ready = gA_hold["ams"]["xn_ready"]


            pe_acquire(4)
            wait("t", s_ve, xn_ready)
            for k in range(KD):
                mm(pb[4][0:8, :], wrt[:, k * 8:(k + 1) * 8], xn[:, k * PT:(k + 1) * PT],
                   start=(k == 0), stop=(k == KD - 1),
                   inc_sem=(pe_inc() if k == KD - 1 else None))
            wait("v", s_pe, s_pe.n)
            m_lg = vop("tensor_copy", lgt[0:8, :], pb[4][0:8, :], inc=True)
            set_guard(4, s_ve)
            pe_acquire(5)
            wait("t", s_ve, m_lg)
            for tc in range(TCH):
                pe_transpose(pb[5][:, tc * 8:(tc + 1) * 8],
                             lgt[0:8, tc * 128:(tc + 1) * 128], id32[0:8, 0:8],
                             inc_sem=(pe_inc() if tc == TCH - 1 else None))
            wait("v", s_pe, s_pe.n)
            m_tm = vop("tensor_copy", lgtm[:, :], pb[5][:, 0:TCH * 8], inc=True)
            set_guard(5, s_ve)
            wait("v", s_ve, m_tm)
            for tc in range(TCH):
                vop("reduce_max", col(0 + tc), lgtm[:, tc * 8:(tc + 1) * 8], axis=AX)
            vbar()
            for tc in range(TCH):
                vop("tensor_scalar", m8[:, tc * 8:(tc + 1) * 8], lgtm[:, tc * 8:(tc + 1) * 8],
                    col(0 + tc), None, ALU.is_ge)
            vbar()
            for tc in range(TCH):
                vop("scalar_tensor_tensor", comb[:, tc * 8:(tc + 1) * 8], m8[:, tc * 8:(tc + 1) * 8],
                    -1e30, lgtm[:, tc * 8:(tc + 1) * 8], ALU.mult, ALU.add)
            vbar()
            for tc in range(TCH):
                vop("reduce_max", col(4 + tc), comb[:, tc * 8:(tc + 1) * 8], axis=AX)
            vbar()
            for tc in range(TCH):
                vop("tensor_scalar", lgtm[:, tc * 8:(tc + 1) * 8], comb[:, tc * 8:(tc + 1) * 8],
                    col(4 + tc), None, ALU.is_ge)
            for tc in range(TCH):
                vop("tensor_tensor", col(8 + tc), col(0 + tc), col(4 + tc), ALU.subtract)
            m_d = vbar()
            wait("s", s_ve, m_d)
            m_th = 0
            for tc in range(TCH):
                m_th = sop("activation", col(12 + tc), col(8 + tc), AF.Tanh, scale=0.5,
                           inc=(tc == TCH - 1))
            wait("v", s_se, m_th)
            for tc in range(TCH):
                vop("tensor_scalar", col(16 + tc), col(12 + tc), 0.5, 0.5, ALU.mult, ALU.add)
                vop("tensor_scalar", col(20 + tc), col(12 + tc), -0.5, 0.5, ALU.mult, ALU.add)
            vbar()
            for tc in range(TCH):
                vop("tensor_scalar", m8[:, tc * 8:(tc + 1) * 8], m8[:, tc * 8:(tc + 1) * 8],
                    col(16 + tc), None, ALU.mult)
            vbar()
            for tc in range(TCH):
                vop("scalar_tensor_tensor", comb[:, tc * 8:(tc + 1) * 8], lgtm[:, tc * 8:(tc + 1) * 8],
                    col(20 + tc), m8[:, tc * 8:(tc + 1) * 8], ALU.mult, ALU.add)
            comb_ready = vbar()

            if ti % (SEQ // PT) == 0:
                gop("memset", halo[:, :], 0.0, inc=True)
                wait("g", s_gp, s_gp.n)
            hg_prev_ms = 0
            for g in range(8):
                tok_ip = ring_load([
                    lambda sl, g=g: (sl[:, 0:4096].rearrange("p (k z c) -> p k z c", k=KD, z=2)[:, :, 0, :],
                                     d_ip.rearrange("(k p) m -> p k m", p=128)[:, :, g * 256:(g + 1) * 256]),
                    lambda sl, g=g: (sl[:, 0:4096].rearrange("p (k z c) -> p k z c", k=KD, z=2)[:, :, 1, :],
                                     d_ip.rearrange("(k p) m -> p k m", p=128)[:, :, I + g * 256:I + (g + 1) * 256]),
                ], 4096)
                slot_ip = tok_ip[2]
                tok_cv = ring_load([
                    lambda sl, g=g, ci=ci: (
                        sl[:, 0:2048].rearrange("p (t z) -> p t z", t=4)[:, :, ci * 256:(ci + 1) * 256],
                        d_conv.rearrange("t p (ci gg c) -> p t ci gg c", ci=2, gg=8)[:, :, ci, g, :])
                    for ci in range(2)
                ], 2048)
                slot_cv = tok_cv[2]
                if hbuf_free is not None:
                    wait("g", hbuf_free[0], hbuf_free[1])
                m_pre = gop("tensor_copy",
                            hbuf[:, :].rearrange("p (ci z) -> p ci z", ci=2)[:, :, 0:3],
                            halo[:, :].rearrange("p (c z) -> p c z", c=KI)[:, 2 * g:2 * g + 2, :],
                            inc=True)
                ring_wait(tok_ip)
                wait("t", s_ve, xn_ready)
                for mc in range(2):
                    pe_acquire(mc)
                    for k in range(KD):
                        mm(pb[mc][:, :], slot_ip[:, k * 512 + mc * 128:k * 512 + (mc + 1) * 128],
                           xn[:, k * PT:(k + 1) * PT], start=(k == 0), stop=(k == KD - 1),
                           inc_sem=(pe_inc() if k == KD - 1 else None))
                ph_done = s_pe.n
                for mc in range(2):
                    pe_acquire(2 + mc)
                    for k in range(KD):
                        last = (k == KD - 1)
                        inc = None
                        if last:
                            inc = ring_free_inc(tok_ip) if mc == 1 else pe_inc()
                        mm(pb[2 + mc][:, :],
                           slot_ip[:, k * 512 + 256 + mc * 128:k * 512 + 256 + (mc + 1) * 128],
                           xn[:, k * PT:(k + 1) * PT], start=(k == 0), stop=last, inc_sem=inc)
                pg_wf = s_wf[tok_ip[0]].n
                wait("s", s_pe, ph_done)
                wait("s", s_gp, m_pre)
                for mc in range(2):
                    sop("copy", hbuf[:, mc * 515 + 3:mc * 515 + 515], pb[mc][:, :],
                        inc=(mc == 1))
                set_guard(0, s_se)
                set_guard(1, s_se)
                h_evac = s_se.n
                wait("s", s_wf[tok_ip[0]], pg_wf)
                wait("s", s_ve, hg_prev_ms)
                for mc in range(2):
                    sop("activation", gt[mc][:, :], pb[2 + mc][:, :], AF.Silu, inc=(mc == 1))
                set_guard(2, s_se)
                set_guard(3, s_se)
                ring_wait(tok_cv)
                wait("t", s_se, h_evac)
                for co in range(2):
                    pe_acquire(4 + co)
                    n_mm = 0
                    for tp in range(4):
                        for ci in range(2):
                            n_mm += 1
                            last = (n_mm == 8)
                            inc = None
                            if last:
                                inc = ring_free_inc(tok_cv) if co == 1 else pe_inc()
                            mm(pb[4 + co][:, :],
                               slot_cv[:, tp * 512 + ci * 256 + co * 128:tp * 512 + ci * 256 + (co + 1) * 128],
                               hbuf[:, ci * 515 + tp:ci * 515 + tp + 512],
                               start=(n_mm == 1), stop=last, inc_sem=inc)
                conv_pe = s_pe.n
                cv_wf = s_wf[tok_cv[0]].n
                hbuf_free = (s_wf[tok_cv[0]], cv_wf)
                wait("v", s_pe, conv_pe)
                wait("v", s_wf[tok_cv[0]], cv_wf)
                wait("v", s_se, s_se.n)
                for co in range(2):
                    ch = 2 * g + co
                    vop("scalar_tensor_tensor", hg[:, ch * PT:(ch + 1) * PT], pb[4 + co][:, :],
                        v_cb[:, ch:ch + 1], gt[co][:, :], ALU.add, ALU.mult, inc=(co == 1))
                set_guard(4, s_ve)
                set_guard(5, s_ve)
                hg_prev_ms = s_ve.n
                wait("g", s_se, h_evac)
                gop("tensor_copy",
                    halo[:, :].rearrange("p (c z) -> p c z", c=KI)[:, 2 * g:2 * g + 2, :],
                    hbuf[:, :].rearrange("p (ci z) -> p ci z", ci=2)[:, :, 512:515],
                    inc=True)
            hg_done = s_ve.n
            hg_last_tile = s_ve.n

            pe_acquire(6)
            wait("t", s_ve, hg_done)
            for c in range(KI):
                mm(pb[6][0:1, :], onesc[:, 0:1], hg[:, c * PT:(c + 1) * PT],
                   start=(c == 0), stop=(c == KI - 1),
                   inc_sem=(pe_inc() if c == KI - 1 else None))
            wait("v", s_pe, s_pe.n)
            m_r0 = vop("tensor_copy", row(0), pb[6][0:1, :], inc=True)
            sq_pe = [0] * KI
            for c in range(KI):
                sp = c % 2
                if c >= 2:
                    wait("s", s_pe, sq_pe[c - 2])
                m_s = sop("square", rt_[sp][:, :], hg[:, c * PT:(c + 1) * PT], inc=True)
                wait("t", s_se, m_s)
                if c == 0:
                    wait("t", s_ve, m_r0)
                mm(pb[6][0:1, :], onesc[:, 0:1], rt_[sp][:, :], start=(c == 0),
                   stop=(c == KI - 1), inc_sem=pe_inc())
                sq_pe[c] = s_pe.n
            wait("v", s_pe, s_pe.n)
            m_r1 = vop("tensor_copy", row(1), pb[6][0:1, :], inc=True)
            set_guard(6, s_ve)
            cst = {"m_ab": 0, "hn_done": 0}

            def g_rows():
                wait("v", s_ve, m_r1)
                m = vop("tensor_scalar", row(2), row(0), 1.0 / I, None, ALU.mult, inc=True)
                wait("v", s_ve, m)
                yield
                m = vop("tensor_scalar", row(3), row(1), 1.0 / I, None, ALU.mult, inc=True)
                wait("v", s_ve, m)
                yield
                m = vop("tensor_tensor", row(6), row(2), row(2), ALU.mult, inc=True)
                wait("v", s_ve, m)
                yield
                m = vop("tensor_tensor", row(7), row(3), row(6), ALU.subtract, inc=True)
                wait("v", s_ve, m)
                yield
                m = vop("tensor_scalar", row(3), row(7), 1e-5, None, ALU.add, inc=True)
                wait("v", s_ve, m)
                yield
                t1, t2, t3 = row(6), row(7), row(5)
                m = vop("tensor_tensor", t1.bitcast(I32), row(3).bitcast(I32), shrow[0:1, :],
                        ALU.logical_shift_right, inc=True)
                wait("v", s_ve, m)
                yield
                m = vop("tensor_scalar", t1.bitcast(I32), t1.bitcast(I32), -1, NEWTON_C,
                        ALU.mult, ALU.add, inc=True)
                wait("v", s_ve, m)
                yield
                cur, nxt = t1, row(4)
                for _ in range(3):
                    m = vop("tensor_tensor", t2, cur, cur, ALU.mult, inc=True)
                    wait("v", s_ve, m)
                    yield
                    m = vop("tensor_tensor", t3, t2, row(3), ALU.mult, inc=True)
                    wait("v", s_ve, m)
                    yield
                    m = vop("tensor_scalar", t3, t3, -0.5, 1.5, ALU.mult, ALU.add, inc=True)
                    wait("v", s_ve, m)
                    yield
                    m = vop("tensor_tensor", nxt, cur, t3, ALU.mult, inc=True)
                    wait("v", s_ve, m)
                    yield
                    cur, nxt = nxt, cur
                m = vop("tensor_tensor", row(6), row(2), row(4), ALU.mult, inc=True)
                wait("v", s_ve, m)
                yield
                cst["m_ab"] = vop("tensor_scalar", row(5), row(6), -1.0, None, ALU.mult, inc=True)
                yield

            def g_hn():
                pe_acquire(6)
                wait("t", s_ve, cst["m_ab"])
                mm(pb[6][:, :], onesr[0:1, :], row(4), start=True, stop=True, inc_sem=pe_inc())
                wait("v", s_pe, s_pe.n)
                vop("tensor_copy", a2sb[:, :], pb[6][:, :], inc=True)
                set_guard(6, s_ve)
                yield
                pe_acquire(6)
                mm(pb[6][:, :], onesr[0:1, :], row(5), start=True, stop=True, inc_sem=pe_inc())
                wait("v", s_pe, s_pe.n)
                m_b2 = vop("tensor_copy", b2sb[:, :], pb[6][:, :], inc=True)
                set_guard(6, s_ve)
                wait("v", s_ve, m_b2)
                yield
                for c0 in range(0, KI, 2):
                    for c in (c0, c0 + 1):
                        vop("tensor_tensor", u1[c % 2][:, :], hg[:, c * PT:(c + 1) * PT],
                            a2sb[:, :], ALU.mult)
                    vbar()
                    yield
                    for c in (c0, c0 + 1):
                        vop("tensor_tensor", e1[c % 2][:, :], u1[c % 2][:, :], b2sb[:, :], ALU.add)
                    vbar()
                    yield
                    for c in (c0, c0 + 1):
                        cst["hn_done"] = vop("tensor_scalar", h2[:, c * PT:(c + 1) * PT],
                                             e1[c % 2][:, :], v_lg[:, c:c + 1], v_lb[:, c:c + 1],
                                             ALU.mult, ALU.add, inc=(c == c0 + 1))
                    vbar()
                    yield

            gens = [g_rows(), g_hn()]

            def drain(ex_idx):
                if gens[0] is not None:
                    try:
                        next(gens[0])
                        return
                    except StopIteration:
                        gens[0] = None
                if ex_idx >= 2 and gens[1] is not None:
                    try:
                        next(gens[1])
                        return
                    except StopIteration:
                        gens[1] = None
            wait("t", s_ve, comb_ready)
            r2 = hn_r2
            for ex in range(NE + 1):
                shared = (ex == NE)
                for half in range(2):
                    m1_toks = []
                    for j in range(4):
                        src_t = d_sfc if shared else d_efc[ex]
                        m1_toks.append(ring_load([
                            lambda sl, j=j, src_t=src_t: (
                                sl[:, 0:4096].rearrange("p (k c) -> p k c", k=KD),
                                src_t.rearrange("(k p) m -> p k m", p=128)[
                                    :, :, half * 2048 + j * 512:half * 2048 + (j + 1) * 512]),
                        ], 4096))
                    r2_ms = [0] * 16
                    for j in range(4):
                        tok_m = m1_toks[j]
                        slot = tok_m[2]
                        ring_wait(tok_m)
                        for mc in range(4):
                            mo = j * 4 + mc
                            bk = mo % 4
                            pe_acquire(bk)
                            for k in range(KD):
                                last = (k == KD - 1)
                                inc = None
                                if last:
                                    inc = ring_free_inc(tok_m) if mc == 3 else pe_inc()
                                mm(pb[bk][:, :], slot[:, k * 512 + mc * 128:k * 512 + (mc + 1) * 128],
                                   xn[:, k * PT:(k + 1) * PT], start=(k == 0), stop=last, inc_sem=inc)
                            if mc == 3:
                                wait("v", s_wf[tok_m[0]], s_wf[tok_m[0]].n)
                            else:
                                wait("v", s_pe, s_pe.n)
                            sp = mo % 2
                            if mo >= 2:
                                wait("v", s_se, r2_ms[mo - 2])
                            m_rl = vop("tensor_scalar", rt_[sp][:, :], pb[bk][:, :], 0.0, None,
                                       ALU.max, inc=True)
                            set_guard(bk, s_ve)
                            wait("s", s_ve, m_rl)
                            r2_ms[mo] = sop("square", r2[:, mo * PT:(mo + 1) * PT],
                                            rt_[sp][:, :], inc=True)
                            drain(ex)
                    mlp1_done = s_se.n
                    src_t = d_spj if shared else d_epj[ex]
                    p_toks = []
                    for n in range(2):
                        p_toks.append(ring_load([
                            lambda sl, n=n, src_t=src_t, half=half: (
                                sl[:, :].rearrange("p (k c) -> p k c", k=KI),
                                src_t.rearrange("(k p) m -> p k m", p=128)[
                                    :, half * KI:(half + 1) * KI, n * 512:(n + 1) * 512]),
                        ], 8192))
                    tok_p0, tok_p1 = p_toks
                    ring_wait(tok_p0)
                    ring_wait(tok_p1)
                    wait("t", s_se, r2_ms[11])
                    for mt in range(TCH):
                        bks = (2, 3) if mt % 2 == 0 else (4, 5)
                        pe_acquire(bks[0])
                        pe_acquire(bks[1])
                        for k in range(KI):
                            if mt == 0 and 12 <= k:
                                wait("t", s_se, r2_ms[k])
                            last = (k == KI - 1)
                            inc0 = inc1 = None
                            if last:
                                if mt == TCH - 1:
                                    inc0 = ring_free_inc(tok_p0)
                                    inc1 = ring_free_inc(tok_p1)
                                else:
                                    inc0 = pe_inc()
                                    inc1 = pe_inc()
                            lhs = r2[:, k * PT + mt * 128:k * PT + (mt + 1) * 128]
                            mm(pb[bks[0]][:, :], lhs, tok_p0[2][:, k * 512:(k + 1) * 512],
                               start=(k == 0), stop=last, inc_sem=inc0)
                            mm(pb[bks[1]][:, :], lhs, tok_p1[2][:, k * 512:(k + 1) * 512],
                               start=(k == 0), stop=last, inc_sem=inc1)
                        if mt == TCH - 1:
                            wait("v", s_wf[tok_p1[0]], s_wf[tok_p1[0]].n)
                            wait("v", s_wf[tok_p0[0]], s_wf[tok_p0[0]].n)
                        else:
                            wait("v", s_pe, s_pe.n)
                        for n in range(2):
                            acc_v = xt[:, mt * D + n * 512:mt * D + (n + 1) * 512]
                            if shared:
                                vop("tensor_tensor", acc_v, pb[bks[n]][:, :], acc_v, ALU.add, inc=True)
                            else:
                                vop("scalar_tensor_tensor", acc_v, pb[bks[n]][:, :],
                                    comb[:, mt * 8 + ex:mt * 8 + ex + 1], acc_v,
                                    ALU.mult, ALU.add, inc=True)
                            set_guard(bks[n], s_ve)
                        drain(ex)
            xn_free_pe = s_pe.n

            while gens[0] is not None or gens[1] is not None:
                drain(99)
            hn_done = cst["hn_done"]

            hg_last_tile = s_ve.n

            if ti + 1 < n_tiles:
                emit_xload(ti + 1)
                ams_next = {}
                gA_hold["gen"] = plan_A(ti + 1, ams_next)
                gA_hold["ams"] = ams_next
            fc_toks = []
            for j in range(4):
                fc_toks.append(ring_load([
                    lambda sl, j=j: (sl[:, 0:4096].rearrange("p (k c) -> p k c", k=KI),
                                     d_fc.rearrange("(k p) m -> p k m", p=128)[:, :, j * 256:(j + 1) * 256]),
                ], 4096))
            wait("t", s_ve, hn_done)
            for j in range(4):
                tok_fc = fc_toks[j]
                slot = tok_fc[2]
                ring_wait(tok_fc)
                for mc in range(2):
                    mg = j * 2 + mc
                    bk = mg % 2
                    pe_acquire(bk)
                    for k in range(KI):
                        last = (k == KI - 1)
                        inc = None
                        if last:
                            inc = ring_free_inc(tok_fc) if mc == 1 else pe_inc()
                        mm(pb[bk][:, :], slot[:, k * 256 + mc * 128:k * 256 + (mc + 1) * 128],
                           h2[:, k * PT:(k + 1) * PT], start=(k == 0), stop=last, inc_sem=inc)
                    if mc == 1:
                        wait("s", s_wf[tok_fc[0]], s_wf[tok_fc[0]].n)
                    else:
                        wait("s", s_pe, s_pe.n)
                    sop("activation", s_sb[:, mg * PT:(mg + 1) * PT], pb[bk][:, :],
                        AF.Silu, bias=v_fb[:, mg:mg + 1], inc=True)
                    set_guard(bk, s_se)
                    drain_A()
            fc_done = s_se.n

            cp_toks = []
            for j in range(4):
                cp_toks.append(ring_load([
                    lambda sl, j=j: (sl[:, 0:4096].rearrange("p (k c) -> p k c", k=KD),
                                     d_cp.rearrange("(k p) m -> p k m", p=128)[:, :, j * 512:(j + 1) * 512]),
                ], 4096))
            wait("t", s_se, fc_done)
            for j in range(4):
                tok_cp = cp_toks[j]
                slot = tok_cp[2]
                ring_wait(tok_cp)
                for mc in range(4):
                    c = j * 4 + mc
                    bk = 2 + c % 2
                    pe_acquire(bk)
                    for k in range(KD):
                        last = (k == KD - 1)
                        inc = None
                        if last:
                            inc = ring_free_inc(tok_cp) if mc == 3 else pe_inc()
                        mm(pb[bk][:, :], slot[:, k * 512 + mc * 128:k * 512 + (mc + 1) * 128],
                           s_sb[:, k * PT:(k + 1) * PT], start=(k == 0), stop=last, inc_sem=inc)
                    if mc == 3:
                        wait("v", s_wf[tok_cp[0]], s_wf[tok_cp[0]].n)
                    else:
                        wait("v", s_pe, s_pe.n)
                    vop("scalar_tensor_tensor", h2[:, c * PT:(c + 1) * PT], pb[bk][:, :],
                        v_pb[:, c:c + 1], hg[:, c * PT:(c + 1) * PT], ALU.add, ALU.add, inc=True)
                    set_guard(bk, s_ve)
                    drain_A()
            h2_done = s_ve.n

            out_toks = []
            for n in range(2):
                out_toks.append(ring_load([
                    lambda sl, n=n: (sl[:, :].rearrange("p (k c) -> p k c", k=KI),
                                     d_out.rearrange("(k p) m -> p k m", p=128)[:, :, n * 512:(n + 1) * 512]),
                ], 8192))
            wait("t", s_ve, h2_done)
            tok_o0, tok_o1 = out_toks
            ring_wait(tok_o0)
            ring_wait(tok_o1)
            for mt in range(TCH):
                bks = (2, 3) if mt % 2 == 0 else (4, 5)
                pe_acquire(bks[0])
                pe_acquire(bks[1])
                for k in range(KI):
                    last = (k == KI - 1)
                    inc0 = inc1 = None
                    if last:
                        if mt == TCH - 1:
                            inc0 = ring_free_inc(tok_o0)
                            inc1 = ring_free_inc(tok_o1)
                        else:
                            inc0 = pe_inc()
                            inc1 = pe_inc()
                    lhs = h2[:, k * PT + mt * 128:k * PT + (mt + 1) * 128]
                    mm(pb[bks[0]][:, :], lhs, tok_o0[2][:, k * 512:(k + 1) * 512],
                       start=(k == 0), stop=last, inc_sem=inc0)
                    mm(pb[bks[1]][:, :], lhs, tok_o1[2][:, k * 512:(k + 1) * 512],
                       start=(k == 0), stop=last, inc_sem=inc1)
                if mt == TCH - 1:
                    wait("v", s_wf[tok_o1[0]], s_wf[tok_o1[0]].n)
                    wait("v", s_wf[tok_o0[0]], s_wf[tok_o0[0]].n)
                else:
                    wait("v", s_pe, s_pe.n)
                for n in range(2):
                    acc_v = xt[:, mt * D + n * 512:mt * D + (n + 1) * 512]
                    vop("tensor_tensor", acc_v, pb[bks[n]][:, :], acc_v, ALU.add, inc=True)
                    set_guard(bks[n], s_ve)
                drain_A()

            acc_done_prev = s_ve.n

            while gA_hold["gen"] is not None:
                drain_A()

            wait("y", s_ve, acc_done_prev)
            em("y", lambda y, a=xt, dst=d_o[ti * PT:(ti + 1) * PT, :]:
               y.dma_start(dst.rearrange("(t p) f -> p t f", p=128),
                           a[:, :].rearrange("p (t f) -> p t f", t=TCH)).then_inc(s_st.h, 16))
            s_st.n += 16
            store_done[ti] = s_st.n

        wait("y", s_st, s_st.n)

        @block.tensor
        def _(t):
            for f in P["t"]:
                f(t)

        @block.vector
        def _(v):
            for f in P["v"]:
                f(v)

        @block.scalar
        def _(s_):
            for f in P["s"]:
                f(s_)

        @block.gpsimd
        def _(g):
            for f in P["g"]:
                f(g)

        @block.sync
        def _(y):
            for f in P["y"]:
                f(y)

    return nc


def _prep_weights(inp):
    f = np.ascontiguousarray
    w = {}
    w["w_ip"] = f(inp["L0_attn_in_proj_w"].T)
    cw = inp["L0_attn_conv_w"].reshape(8, 2, 128, 2, 128, 4)
    w["w_conv"] = f(cw.transpose(5, 4, 3, 0, 1, 2).reshape(4, 128, 4096))
    w["w_fc"] = f(inp["L0_attn_fc_w"].T)
    w["w_cproj"] = f(inp["L0_attn_cproj_w"].T)
    w["w_out"] = f(inp["L0_attn_out_w"].T)
    w["w_router"] = f(inp["L0_ffn_router_w"].T)
    w["w_efc"] = f(inp["L0_ffn_w_fc"].transpose(0, 2, 1))
    w["w_eproj"] = f(inp["L0_ffn_w_proj"].transpose(0, 2, 1))
    w["w_sfc"] = f(inp["L0_ffn_shared_fc"].T)
    w["w_sproj"] = f(inp["L0_ffn_shared_proj"].T)
    w["b_conv"] = f(inp["L0_attn_conv_b"].reshape(16, 128).T)
    w["b_lng"] = f(inp["L0_attn_ln_g"].reshape(16, 128).T)
    w["b_lnb"] = f(inp["L0_attn_ln_b"].reshape(16, 128).T)
    w["b_fc"] = f(inp["L0_attn_fc_b"].reshape(8, 128).T)
    w["b_cproj"] = f(inp["L0_attn_cproj_b"].reshape(16, 128).T)
    w["ident"] = np.eye(128, dtype=np.float32)
    return w


def kernel(**inputs):
    x_full = np.asarray(inputs["L0_attn_hidden_states"], dtype=np.float32)
    w = _prep_weights({k: np.asarray(v, dtype=np.float32) for k, v in inputs.items()
                       if k != "L0_attn_hidden_states"})
    nc = build_nc(16)
    in_maps = []
    for c in range(NCORES):
        m = dict(w)
        m["x"] = np.ascontiguousarray(x_full[c * BPC:(c + 1) * BPC].reshape(TOK, D))
        in_maps.append(m)
    res = run_bass_kernel_spmd(nc, in_maps, core_ids=list(range(NCORES)))
    outs = [r["out"].reshape(BPC, SEQ, D) for r in res.results]
    return np.concatenate(outs, axis=0)


# version 11 (final)
# speedup vs baseline: 1.1752x; 1.1752x over previous
import numpy as np
from contextlib import ExitStack

import concourse.bass as bass
import concourse.mybir as mybir
from concourse.bass_utils import run_bass_kernel_spmd

import concourse.bass_utils as _bu

if not getattr(_bu, "_ldw_opt_patched", False):
    _orig_run_command = _bu.run_command

    def _run_command_ldw(cmd, **kw):
        if isinstance(cmd, list):
            cmd = ["--enable-ldw-opt=true" if c == "--enable-ldw-opt=false" else c
                   for c in cmd]
        return _orig_run_command(cmd, **kw)

    _bu.run_command = _run_command_ldw
    _bu._ldw_opt_patched = True


F32 = mybir.dt.float32
BF16 = mybir.dt.bfloat16
I32 = mybir.dt.int32
AF = mybir.ActivationFunctionType
ALU = mybir.AluOpType
AX = mybir.AxisListType.X

D = 1024
I = 2048
DFF = 4096
NE = 8
NCORES = 8
BPC = 4
SEQ = 2048
TOK = BPC * SEQ
PT = 512
TCH = PT // 128
KD = D // 128
KI = I // 128
KF = DFF // 128
NEWTON_C = 0x5F3759DF
NSLOT = 6


class Sem:
    def __init__(self, h):
        self.h = h
        self.n = 0

    def bump(self, k=1):
        self.n += k
        return self.n


def build_nc(n_tiles=16):
    nc = bass.Bass("TRN2", target_bir_lowering=False, debug=False)

    d_x = nc.dram_tensor("x", [TOK, D], F32, kind="ExternalInput").ap()
    d_ip = nc.dram_tensor("w_ip", [D, 2 * I], F32, kind="ExternalInput").ap()
    d_conv = nc.dram_tensor("w_conv", [4, 128, 4096], F32, kind="ExternalInput").ap()
    d_fc = nc.dram_tensor("w_fc", [I, D], F32, kind="ExternalInput").ap()
    d_cp = nc.dram_tensor("w_cproj", [D, I], F32, kind="ExternalInput").ap()
    d_out = nc.dram_tensor("w_out", [I, D], F32, kind="ExternalInput").ap()
    d_rt = nc.dram_tensor("w_router", [D, NE], F32, kind="ExternalInput").ap()
    d_efc = nc.dram_tensor("w_efc", [NE, D, DFF], F32, kind="ExternalInput").ap()
    d_epj = nc.dram_tensor("w_eproj", [NE, DFF, D], F32, kind="ExternalInput").ap()
    d_sfc = nc.dram_tensor("w_sfc", [D, DFF], F32, kind="ExternalInput").ap()
    d_spj = nc.dram_tensor("w_sproj", [DFF, D], F32, kind="ExternalInput").ap()
    d_cb = nc.dram_tensor("b_conv", [128, KI], F32, kind="ExternalInput").ap()
    d_lg = nc.dram_tensor("b_lng", [128, KI], F32, kind="ExternalInput").ap()
    d_lb = nc.dram_tensor("b_lnb", [128, KI], F32, kind="ExternalInput").ap()
    d_fb = nc.dram_tensor("b_fc", [128, KD], F32, kind="ExternalInput").ap()
    d_pb = nc.dram_tensor("b_cproj", [128, KI], F32, kind="ExternalInput").ap()
    d_id = nc.dram_tensor("ident", [128, 128], F32, kind="ExternalInput").ap()
    d_o = nc.dram_tensor("out", [TOK, D], F32, kind="ExternalOutput").ap()
    d_mir = nc.dram_tensor("wmir", [128, 688128], BF16, kind="Internal").ap()

    with ExitStack() as es:
        E = es.enter_context
        x_t = [E(nc.sbuf_tensor(f"x{p}", [128, TCH * D], F32)) for p in range(2)]
        xn = E(nc.sbuf_tensor("xn", [128, KD * PT], BF16))
        hbuf = E(nc.sbuf_tensor("hbuf", [128, 2 * 515], BF16))
        halo = E(nc.sbuf_tensor("halo", [128, KI * 3], BF16))
        hg = E(nc.sbuf_tensor("hg", [128, KI * PT], F32))
        hn_r2 = E(nc.sbuf_tensor("hn_r2", [128, KI * PT], BF16))
        s_sb = E(nc.sbuf_tensor("s_sb", [128, KD * PT], BF16))
        h2 = E(nc.sbuf_tensor("h2", [128, KI * PT], BF16))
        wslot = [E(nc.sbuf_tensor(f"ws{i}", [128, 4096], BF16)) for i in range(NSLOT)]
        xns = [E(nc.sbuf_tensor(f"xns{p}", [128, D], BF16)) for p in range(2)]
        gt = [E(nc.sbuf_tensor(f"gt{p}", [128, PT], F32)) for p in range(2)]
        ct = [E(nc.sbuf_tensor(f"ct{p}", [128, PT], F32)) for p in range(2)]
        rt_ = [E(nc.sbuf_tensor(f"rt{p}", [128, PT], F32)) for p in range(2)]
        u1 = gt
        e1 = ct
        a2sb = E(nc.sbuf_tensor("a2sb", [128, PT], F32))
        b2sb = E(nc.sbuf_tensor("b2sb", [128, PT], F32))
        rows = E(nc.sbuf_tensor("rows", [1, 8 * PT], F32))
        cols = E(nc.sbuf_tensor("cols", [128, 40], F32))
        bnt4 = E(nc.sbuf_tensor("bnt4", [128, 48], F32))
        bna4 = E(nc.sbuf_tensor("bna4", [128, 8], F32))
        sh4 = E(nc.sbuf_tensor("sh4", [128, 4], I32))
        lgt = E(nc.sbuf_tensor("lgt", [8, PT], F32))
        lgtm = E(nc.sbuf_tensor("lgtm", [128, TCH * 8], F32))
        comb = E(nc.sbuf_tensor("comb", [128, TCH * 8], F32))
        m8 = E(nc.sbuf_tensor("m8", [128, TCH * 8], F32))
        wrt = E(nc.sbuf_tensor("wrt", [128, KD * 8], BF16))
        biases = E(nc.sbuf_tensor("biases", [128, 4 * KI + KD], F32))
        id32 = E(nc.sbuf_tensor("id32", [128, 128], F32))
        id16 = E(nc.sbuf_tensor("id16", [128, 128], BF16))
        onesc = E(nc.sbuf_tensor("onesc", [128, 1], F32))
        onesr = E(nc.sbuf_tensor("onesr", [1, 128], F32))
        sh1 = E(nc.sbuf_tensor("sh1", [128, 1], I32))
        shrow = E(nc.sbuf_tensor("shrow", [1, PT], I32))

        pb = [E(nc.psum_tensor(f"pb{i}", [128, 512], F32)) for i in range(7)]
        pt16 = E(nc.psum_tensor("pt16", [128, 512], BF16))

        s_pe = Sem(E(nc.semaphore("s_pe")))
        s_ve = Sem(E(nc.semaphore("s_ve")))
        s_se = Sem(E(nc.semaphore("s_se")))
        s_gp = Sem(E(nc.semaphore("s_gp")))
        s_x = Sem(E(nc.semaphore("s_x")))
        s_st = Sem(E(nc.semaphore("s_st")))
        s_gd = Sem(E(nc.semaphore("s_gd")))
        s_wl = [Sem(E(nc.semaphore(f"s_wl{i}"))) for i in range(NSLOT)]
        s_wf = [Sem(E(nc.semaphore(f"s_wf{i}"))) for i in range(NSLOT)]

        block = E(nc.Block())

        P = {"t": [], "v": [], "s": [], "g": [], "y": []}

        def em(key, fn):
            P[key].append(fn)

        def wait(key, sem, val):
            if val > 0:
                em(key, lambda e, s=sem.h, v=val: e.wait_ge(s, v))

        guards = {}

        def pe_acquire(bk):
            g = guards.pop(bk, None)
            if g is not None:
                wait("t", g[0], g[1])

        def set_guard(bk, sem):
            guards[bk] = (sem, sem.n)

        ring_next = [0]
        ring_loads = [0] * NSLOT
        ring_uses = [0] * NSLOT
        cur = {"ti": 0, "j": 0, "off": 0}
        mir_offs = []
        mir_store_ms = []
        mir_pending = [0] * NSLOT

        def ring_load(dma_fn_list, cols):
            i = ring_next[0]
            ring_next[0] = (i + 1) % NSLOT
            j = cur["j"]
            cur["j"] += 1
            if cur["ti"] == 0:
                mir_offs.append(cur["off"])
                cur["off"] += cols
            off = mir_offs[j]
            slot = wslot[i]
            if cur["ti"] == 0:
                if ring_uses[i] > 0:
                    wait("g", s_wf[i], ring_uses[i])
                    if mir_pending[i]:
                        wait("g", s_st, mir_pending[i])
                for fn in dma_fn_list:
                    dst, src = fn(slot)
                    ring_loads[i] += 1
                    s_wl[i].n = 16 * ring_loads[i]
                    em("g", lambda g, d=dst, s=src, sm=s_wl[i].h: g.dma_start(d, s).then_inc(sm, 16))
                tgt = 16 * ring_loads[i]
                wait("y", s_wl[i], tgt)
                s_st.n += 16
                em("y", lambda y, d=d_mir[:, off:off + cols], s=slot[:, 0:cols], sm=s_st.h:
                   y.dma_start(d, s).then_inc(sm, 16))
                mir_pending[i] = s_st.n
                mir_store_ms.append(s_st.n)
            else:
                q = "y" if (j % 2 == 0) else "g"
                if q == "g" and cur["ti"] == 1:
                    wait(q, s_st, mir_store_ms[j])
                wait(q, s_wf[i], ring_uses[i])
                ring_loads[i] += 1
                tgt = 16 * ring_loads[i]
                s_wl[i].n = tgt
                em(q, lambda e, d=slot[:, 0:cols], s=d_mir[:, off:off + cols], sm=s_wl[i].h:
                   e.dma_start(d, s).then_inc(sm, 16))
            ring_uses[i] += 1
            return (i, 16 * ring_loads[i], slot)

        def ring_wait(tok):
            wait("t", s_wl[tok[0]], tok[1])

        def ring_free_inc(tok):
            s_wf[tok[0]].bump()
            return s_wf[tok[0]].h

        def mm(out_ap, lhs_ap, rhs_ap, start, stop, inc_sem=None):
            if inc_sem is None:
                em("t", lambda t, o=out_ap, l=lhs_ap, r=rhs_ap, a=start, b=stop:
                   t.matmul(o, l, r, start=a, stop=b))
            else:
                em("t", lambda t, o=out_ap, l=lhs_ap, r=rhs_ap, a=start, b=stop, s=inc_sem:
                   t.matmul(o, l, r, start=a, stop=b).then_inc(s, 1))

        def pe_transpose(out_ap, in_ap, ident, inc_sem=None):
            if inc_sem is None:
                em("t", lambda t, o=out_ap, i=in_ap, d=ident: t.transpose(o, i, d))
            else:
                em("t", lambda t, o=out_ap, i=in_ap, d=ident, s=inc_sem:
                   t.transpose(o, i, d).then_inc(s, 1))

        def vop(name, *aps, inc=False, **kw):
            if inc:
                s_ve.bump()
                em("v", lambda v, n=name, a=aps, k=kw, s=s_ve.h: getattr(v, n)(*a, **k).then_inc(s, 1))
            else:
                em("v", lambda v, n=name, a=aps, k=kw: getattr(v, n)(*a, **k))
            return s_ve.n

        def sop(name, *aps, inc=False, **kw):
            if inc:
                s_se.bump()
                em("s", lambda s_, n=name, a=aps, k=kw, s=s_se.h: getattr(s_, n)(*a, **k).then_inc(s, 1))
            else:
                em("s", lambda s_, n=name, a=aps, k=kw: getattr(s_, n)(*a, **k))
            return s_se.n

        def gop(name, *aps, inc=False, **kw):
            if inc:
                s_gp.bump()
                em("g", lambda g, n=name, a=aps, k=kw, s=s_gp.h: getattr(g, n)(*a, **k).then_inc(s, 1))
            else:
                em("g", lambda g, n=name, a=aps, k=kw: getattr(g, n)(*a, **k))
            return s_gp.n

        def pe_inc():
            s_pe.bump()
            return s_pe.h

        def vbar():
            m = vop("tensor_copy", cols[:, 37:38], cols[:, 36:37], inc=True)
            wait("v", s_ve, m)
            return m

        v_cb = biases[:, 0:KI]
        v_lg = biases[:, KI:2 * KI]
        v_lb = biases[:, 2 * KI:3 * KI]
        v_pb = biases[:, 3 * KI:4 * KI]
        v_fb = biases[:, 4 * KI:4 * KI + KD]

        def row(i):
            return rows[0:1, i * PT:(i + 1) * PT]

        def col(i):
            return cols[:, i:i + 1]

        for dst, src in [(biases[:, 0:KI], d_cb), (biases[:, KI:2 * KI], d_lg),
                         (biases[:, 2 * KI:3 * KI], d_lb), (biases[:, 3 * KI:4 * KI], d_pb),
                         (biases[:, 4 * KI:4 * KI + KD], d_fb), (id32[:, :], d_id)]:
            em("y", lambda y, d=dst, s=src: y.dma_start(d, s).then_inc(s_x.h, 16))
            s_x.n += 16
        em("g", lambda g: g.dma_start(wrt[:, :].rearrange("p (k e) -> p k e", k=KD),
                                      d_rt.rearrange("(k p) e -> p k e", p=128)).then_inc(s_gd.h, 16))
        em("g", lambda g: g.dma_start(id16[:, :], d_id).then_inc(s_gd.h, 16))
        s_gd.n += 32
        CONST_GP = s_gd.n
        CONST_SY = s_x.n
        wait("v", s_x, CONST_SY)
        vop("memset", onesc[:, :], 1.0)
        vop("memset", onesr[:, :], 1.0)
        vop("memset", sh1[:, :], 1)
        vop("memset", sh4[:, :], 1)
        vop("memset", cols[:, :], 0.0)
        CONST_VE = vop("memset", shrow[:, :], 1, inc=True)

        x_loaded = [0] * n_tiles
        store_done = [0] * n_tiles

        xn_free_pe = 0
        hg_last_tile = 0
        hbuf_free = None
        acc_done_prev = 0

        def newton_rsqrt(dst, src, t1, t2, t3, shift_ap):
            m = vop("tensor_tensor", t1.bitcast(I32), src.bitcast(I32), shift_ap,
                    ALU.logical_shift_right, inc=True)
            wait("v", s_ve, m)
            m = vop("tensor_scalar", t1.bitcast(I32), t1.bitcast(I32), -1, NEWTON_C,
                    ALU.mult, ALU.add, inc=True)
            wait("v", s_ve, m)
            cur, nxt = t1, dst
            for _ in range(3):
                m = vop("tensor_tensor", t2, cur, cur, ALU.mult, inc=True)
                wait("v", s_ve, m)
                m = vop("tensor_tensor", t3, t2, src, ALU.mult, inc=True)
                wait("v", s_ve, m)
                m = vop("tensor_scalar", t3, t3, -0.5, 1.5, ALU.mult, ALU.add, inc=True)
                wait("v", s_ve, m)
                m = vop("tensor_tensor", nxt, cur, t3, ALU.mult, inc=True)
                wait("v", s_ve, m)
                cur, nxt = nxt, cur
            if cur is not dst:
                m = vop("tensor_copy", dst, cur, inc=True)
                wait("v", s_ve, m)

        def plan_A(ati, ams):
            axt = x_t[ati % 2]
            wait("v", s_x, x_loaded[ati])
            wait("v", s_ve, hg_last_tile)
            for tc in range(TCH):
                xv = axt[:, tc * D:(tc + 1) * D]
                vop("bn_stats", bnt4[:, 12 * tc:12 * tc + 6], xv[:, 0:PT])
                vop("bn_stats", bnt4[:, 12 * tc + 6:12 * tc + 12], xv[:, PT:2 * PT])
                yield
            vbar()
            for tc in range(TCH):
                vop("bn_aggr", bna4[:, 2 * tc:2 * tc + 2], bnt4[:, 12 * tc:12 * tc + 12])
            vbar()
            yield
            mean_v = bna4[:, :].rearrange("p (t z) -> p t z", z=2)[:, :, 0]
            var_v = bna4[:, :].rearrange("p (t z) -> p t z", z=2)[:, :, 1]
            m = vop("tensor_scalar", cols[:, 12:16], var_v, 1e-5, None, ALU.add, inc=True)
            wait("v", s_ve, m)
            yield
            newton_rsqrt(cols[:, 16:20], cols[:, 12:16], cols[:, 24:28], cols[:, 28:32],
                         cols[:, 32:36], sh4[:, :])
            yield
            m = vop("tensor_tensor", cols[:, 32:36], mean_v, cols[:, 16:20], ALU.mult, inc=True)
            wait("v", s_ve, m)
            m = vop("tensor_scalar", cols[:, 20:24], cols[:, 32:36], -1.0, None, ALU.mult, inc=True)
            wait("v", s_ve, m)
            yield
            tr_ms = [0] * TCH
            for tc in range(TCH):
                sp = tc % 2
                if tc >= 2:
                    wait("v", s_pe, tr_ms[tc - 2])
                m_st = vop("tensor_scalar", xns[sp][:, :], axt[:, tc * D:(tc + 1) * D],
                           col(16 + tc), col(20 + tc), ALU.mult, ALU.add, inc=True)
                wait("t", s_ve, m_st)
                if ati == 0 and tc == 0:
                    wait("t", s_gd, CONST_GP)
                if ati > 0 and tc == 0:
                    wait("t", s_pe, xn_free_pe)
                yield
                for half in range(2):
                    pe_acquire("pt16")
                    for q in range(4):
                        c = half * 4 + q
                        pe_transpose(pt16[:, q * 128:(q + 1) * 128],
                                     xns[sp][:, c * 128:(c + 1) * 128], id16[:, :],
                                     inc_sem=(pe_inc() if q == 3 else None))
                    wait("v", s_pe, s_pe.n)
                    xnv = xn[:, :].rearrange("p (k t) -> p k t", k=KD)[
                        :, half * 4:half * 4 + 4, tc * 128:(tc + 1) * 128]
                    vop("tensor_copy", xnv,
                        pt16[:, 0:512].rearrange("p (q c) -> p q c", q=4), inc=True)
                    set_guard("pt16", s_ve)
                    yield
                tr_ms[tc] = s_pe.n
            ams["xn_ready"] = s_ve.n

        gA_hold = {"gen": None, "ams": None}

        def drain_A():
            g = gA_hold["gen"]
            if g is not None:
                try:
                    next(g)
                except StopIteration:
                    gA_hold["gen"] = None

        def emit_xload(ti2):
            if ti2 >= 2:
                wait("y", s_st, store_done[ti2 - 2])
            a2 = x_t[ti2 % 2]
            em("y", lambda y, a=a2, src=d_x[ti2 * PT:(ti2 + 1) * PT, :]:
               y.dma_start(a[:, :].rearrange("p (t f) -> p t f", t=TCH),
                           src.rearrange("(t p) f -> p t f", p=128)).then_inc(s_x.h, 16))
            s_x.n += 16
            x_loaded[ti2] = s_x.n

        for ti in range(n_tiles):
            cur["ti"] = ti
            cur["j"] = 0
            par = ti % 2
            xt = x_t[par]

            if ti == 0:
                emit_xload(0)
                ams0 = {}
                for _ in plan_A(0, ams0):
                    pass
                xn_ready = ams0["xn_ready"]
            else:
                xn_ready = gA_hold["ams"]["xn_ready"]


            pe_acquire(4)
            wait("t", s_ve, xn_ready)
            for k in range(KD):
                mm(pb[4][0:8, :], wrt[:, k * 8:(k + 1) * 8], xn[:, k * PT:(k + 1) * PT],
                   start=(k == 0), stop=(k == KD - 1),
                   inc_sem=(pe_inc() if k == KD - 1 else None))
            wait("v", s_pe, s_pe.n)
            m_lg = vop("tensor_copy", lgt[0:8, :], pb[4][0:8, :], inc=True)
            set_guard(4, s_ve)
            pe_acquire(5)
            wait("t", s_ve, m_lg)
            for tc in range(TCH):
                pe_transpose(pb[5][:, tc * 8:(tc + 1) * 8],
                             lgt[0:8, tc * 128:(tc + 1) * 128], id32[0:8, 0:8],
                             inc_sem=(pe_inc() if tc == TCH - 1 else None))
            wait("v", s_pe, s_pe.n)
            m_tm = vop("tensor_copy", lgtm[:, :], pb[5][:, 0:TCH * 8], inc=True)
            set_guard(5, s_ve)
            wait("v", s_ve, m_tm)
            for tc in range(TCH):
                vop("reduce_max", col(0 + tc), lgtm[:, tc * 8:(tc + 1) * 8], axis=AX)
            vbar()
            for tc in range(TCH):
                vop("tensor_scalar", m8[:, tc * 8:(tc + 1) * 8], lgtm[:, tc * 8:(tc + 1) * 8],
                    col(0 + tc), None, ALU.is_ge)
            vbar()
            for tc in range(TCH):
                vop("scalar_tensor_tensor", comb[:, tc * 8:(tc + 1) * 8], m8[:, tc * 8:(tc + 1) * 8],
                    -1e30, lgtm[:, tc * 8:(tc + 1) * 8], ALU.mult, ALU.add)
            vbar()
            for tc in range(TCH):
                vop("reduce_max", col(4 + tc), comb[:, tc * 8:(tc + 1) * 8], axis=AX)
            vbar()
            for tc in range(TCH):
                vop("tensor_scalar", lgtm[:, tc * 8:(tc + 1) * 8], comb[:, tc * 8:(tc + 1) * 8],
                    col(4 + tc), None, ALU.is_ge)
            for tc in range(TCH):
                vop("tensor_tensor", col(8 + tc), col(0 + tc), col(4 + tc), ALU.subtract)
            m_d = vbar()
            wait("s", s_ve, m_d)
            m_th = 0
            for tc in range(TCH):
                m_th = sop("activation", col(12 + tc), col(8 + tc), AF.Tanh, scale=0.5,
                           inc=(tc == TCH - 1))
            wait("v", s_se, m_th)
            for tc in range(TCH):
                vop("tensor_scalar", col(16 + tc), col(12 + tc), 0.5, 0.5, ALU.mult, ALU.add)
                vop("tensor_scalar", col(20 + tc), col(12 + tc), -0.5, 0.5, ALU.mult, ALU.add)
            vbar()
            for tc in range(TCH):
                vop("tensor_scalar", m8[:, tc * 8:(tc + 1) * 8], m8[:, tc * 8:(tc + 1) * 8],
                    col(16 + tc), None, ALU.mult)
            vbar()
            for tc in range(TCH):
                vop("scalar_tensor_tensor", comb[:, tc * 8:(tc + 1) * 8], lgtm[:, tc * 8:(tc + 1) * 8],
                    col(20 + tc), m8[:, tc * 8:(tc + 1) * 8], ALU.mult, ALU.add)
            comb_ready = vbar()

            if ti % (SEQ // PT) == 0:
                gop("memset", halo[:, :], 0.0, inc=True)
                wait("g", s_gp, s_gp.n)
            hg_prev_ms = 0
            for g in range(8):
                tok_ip = ring_load([
                    lambda sl, g=g: (sl[:, 0:4096].rearrange("p (k z c) -> p k z c", k=KD, z=2)[:, :, 0, :],
                                     d_ip.rearrange("(k p) m -> p k m", p=128)[:, :, g * 256:(g + 1) * 256]),
                    lambda sl, g=g: (sl[:, 0:4096].rearrange("p (k z c) -> p k z c", k=KD, z=2)[:, :, 1, :],
                                     d_ip.rearrange("(k p) m -> p k m", p=128)[:, :, I + g * 256:I + (g + 1) * 256]),
                ], 4096)
                slot_ip = tok_ip[2]
                tok_cv = ring_load([
                    lambda sl, g=g, ci=ci: (
                        sl[:, 0:2048].rearrange("p (t z) -> p t z", t=4)[:, :, ci * 256:(ci + 1) * 256],
                        d_conv.rearrange("t p (ci gg c) -> p t ci gg c", ci=2, gg=8)[:, :, ci, g, :])
                    for ci in range(2)
                ], 2048)
                slot_cv = tok_cv[2]
                if hbuf_free is not None:
                    wait("g", hbuf_free[0], hbuf_free[1])
                m_pre = gop("tensor_copy",
                            hbuf[:, :].rearrange("p (ci z) -> p ci z", ci=2)[:, :, 0:3],
                            halo[:, :].rearrange("p (c z) -> p c z", c=KI)[:, 2 * g:2 * g + 2, :],
                            inc=True)
                ring_wait(tok_ip)
                wait("t", s_ve, xn_ready)
                for mc in range(2):
                    pe_acquire(mc)
                    for k in range(KD):
                        mm(pb[mc][:, :], slot_ip[:, k * 512 + mc * 128:k * 512 + (mc + 1) * 128],
                           xn[:, k * PT:(k + 1) * PT], start=(k == 0), stop=(k == KD - 1),
                           inc_sem=(pe_inc() if k == KD - 1 else None))
                ph_done = s_pe.n
                for mc in range(2):
                    pe_acquire(2 + mc)
                    for k in range(KD):
                        last = (k == KD - 1)
                        inc = None
                        if last:
                            inc = ring_free_inc(tok_ip) if mc == 1 else pe_inc()
                        mm(pb[2 + mc][:, :],
                           slot_ip[:, k * 512 + 256 + mc * 128:k * 512 + 256 + (mc + 1) * 128],
                           xn[:, k * PT:(k + 1) * PT], start=(k == 0), stop=last, inc_sem=inc)
                pg_wf = s_wf[tok_ip[0]].n
                wait("s", s_pe, ph_done)
                wait("s", s_gp, m_pre)
                for mc in range(2):
                    sop("copy", hbuf[:, mc * 515 + 3:mc * 515 + 515], pb[mc][:, :],
                        inc=(mc == 1))
                set_guard(0, s_se)
                set_guard(1, s_se)
                h_evac = s_se.n
                wait("s", s_wf[tok_ip[0]], pg_wf)
                wait("s", s_ve, hg_prev_ms)
                for mc in range(2):
                    sop("activation", gt[mc][:, :], pb[2 + mc][:, :], AF.Silu, inc=(mc == 1))
                set_guard(2, s_se)
                set_guard(3, s_se)
                ring_wait(tok_cv)
                wait("t", s_se, h_evac)
                for co in range(2):
                    pe_acquire(4 + co)
                    n_mm = 0
                    for tp in range(4):
                        for ci in range(2):
                            n_mm += 1
                            last = (n_mm == 8)
                            inc = None
                            if last:
                                inc = ring_free_inc(tok_cv) if co == 1 else pe_inc()
                            mm(pb[4 + co][:, :],
                               slot_cv[:, tp * 512 + ci * 256 + co * 128:tp * 512 + ci * 256 + (co + 1) * 128],
                               hbuf[:, ci * 515 + tp:ci * 515 + tp + 512],
                               start=(n_mm == 1), stop=last, inc_sem=inc)
                conv_pe = s_pe.n
                cv_wf = s_wf[tok_cv[0]].n
                hbuf_free = (s_wf[tok_cv[0]], cv_wf)
                wait("v", s_pe, conv_pe)
                wait("v", s_wf[tok_cv[0]], cv_wf)
                wait("v", s_se, s_se.n)
                for co in range(2):
                    ch = 2 * g + co
                    vop("scalar_tensor_tensor", hg[:, ch * PT:(ch + 1) * PT], pb[4 + co][:, :],
                        v_cb[:, ch:ch + 1], gt[co][:, :], ALU.add, ALU.mult, inc=(co == 1))
                set_guard(4, s_ve)
                set_guard(5, s_ve)
                hg_prev_ms = s_ve.n
                wait("g", s_se, h_evac)
                gop("tensor_copy",
                    halo[:, :].rearrange("p (c z) -> p c z", c=KI)[:, 2 * g:2 * g + 2, :],
                    hbuf[:, :].rearrange("p (ci z) -> p ci z", ci=2)[:, :, 512:515],
                    inc=True)
            hg_done = s_ve.n
            hg_last_tile = s_ve.n

            pe_acquire(6)
            wait("t", s_ve, hg_done)
            for c in range(KI):
                mm(pb[6][0:1, :], onesc[:, 0:1], hg[:, c * PT:(c + 1) * PT],
                   start=(c == 0), stop=(c == KI - 1),
                   inc_sem=(pe_inc() if c == KI - 1 else None))
            wait("v", s_pe, s_pe.n)
            m_r0 = vop("tensor_copy", row(0), pb[6][0:1, :], inc=True)
            sq_pe = [0] * KI
            for c in range(KI):
                sp = c % 2
                if c >= 2:
                    wait("s", s_pe, sq_pe[c - 2])
                m_s = sop("square", rt_[sp][:, :], hg[:, c * PT:(c + 1) * PT], inc=True)
                wait("t", s_se, m_s)
                if c == 0:
                    wait("t", s_ve, m_r0)
                mm(pb[6][0:1, :], onesc[:, 0:1], rt_[sp][:, :], start=(c == 0),
                   stop=(c == KI - 1), inc_sem=pe_inc())
                sq_pe[c] = s_pe.n
            wait("v", s_pe, s_pe.n)
            m_r1 = vop("tensor_copy", row(1), pb[6][0:1, :], inc=True)
            set_guard(6, s_ve)
            cst = {"m_ab": 0, "hn_done": 0}

            def g_rows():
                wait("v", s_ve, m_r1)
                m = vop("tensor_scalar", row(2), row(0), 1.0 / I, None, ALU.mult, inc=True)
                wait("v", s_ve, m)
                yield
                m = vop("tensor_scalar", row(3), row(1), 1.0 / I, None, ALU.mult, inc=True)
                wait("v", s_ve, m)
                yield
                m = vop("tensor_tensor", row(6), row(2), row(2), ALU.mult, inc=True)
                wait("v", s_ve, m)
                yield
                m = vop("tensor_tensor", row(7), row(3), row(6), ALU.subtract, inc=True)
                wait("v", s_ve, m)
                yield
                m = vop("tensor_scalar", row(3), row(7), 1e-5, None, ALU.add, inc=True)
                wait("v", s_ve, m)
                yield
                t1, t2, t3 = row(6), row(7), row(5)
                m = vop("tensor_tensor", t1.bitcast(I32), row(3).bitcast(I32), shrow[0:1, :],
                        ALU.logical_shift_right, inc=True)
                wait("v", s_ve, m)
                yield
                m = vop("tensor_scalar", t1.bitcast(I32), t1.bitcast(I32), -1, NEWTON_C,
                        ALU.mult, ALU.add, inc=True)
                wait("v", s_ve, m)
                yield
                cur, nxt = t1, row(4)
                for _ in range(3):
                    m = vop("tensor_tensor", t2, cur, cur, ALU.mult, inc=True)
                    wait("v", s_ve, m)
                    yield
                    m = vop("tensor_tensor", t3, t2, row(3), ALU.mult, inc=True)
                    wait("v", s_ve, m)
                    yield
                    m = vop("tensor_scalar", t3, t3, -0.5, 1.5, ALU.mult, ALU.add, inc=True)
                    wait("v", s_ve, m)
                    yield
                    m = vop("tensor_tensor", nxt, cur, t3, ALU.mult, inc=True)
                    wait("v", s_ve, m)
                    yield
                    cur, nxt = nxt, cur
                m = vop("tensor_tensor", row(6), row(2), row(4), ALU.mult, inc=True)
                wait("v", s_ve, m)
                yield
                cst["m_ab"] = vop("tensor_scalar", row(5), row(6), -1.0, None, ALU.mult, inc=True)
                yield

            def g_hn():
                pe_acquire(6)
                wait("t", s_ve, cst["m_ab"])
                mm(pb[6][:, :], onesr[0:1, :], row(4), start=True, stop=True, inc_sem=pe_inc())
                wait("v", s_pe, s_pe.n)
                vop("tensor_copy", a2sb[:, :], pb[6][:, :], inc=True)
                set_guard(6, s_ve)
                yield
                pe_acquire(6)
                mm(pb[6][:, :], onesr[0:1, :], row(5), start=True, stop=True, inc_sem=pe_inc())
                wait("v", s_pe, s_pe.n)
                m_b2 = vop("tensor_copy", b2sb[:, :], pb[6][:, :], inc=True)
                set_guard(6, s_ve)
                wait("v", s_ve, m_b2)
                yield
                for c0 in range(0, KI, 2):
                    for c in (c0, c0 + 1):
                        vop("tensor_tensor", u1[c % 2][:, :], hg[:, c * PT:(c + 1) * PT],
                            a2sb[:, :], ALU.mult)
                    vbar()
                    yield
                    for c in (c0, c0 + 1):
                        vop("tensor_tensor", e1[c % 2][:, :], u1[c % 2][:, :], b2sb[:, :], ALU.add)
                    vbar()
                    yield
                    for c in (c0, c0 + 1):
                        cst["hn_done"] = vop("tensor_scalar", h2[:, c * PT:(c + 1) * PT],
                                             e1[c % 2][:, :], v_lg[:, c:c + 1], v_lb[:, c:c + 1],
                                             ALU.mult, ALU.add, inc=(c == c0 + 1))
                    vbar()
                    yield

            gens = [g_rows(), g_hn()]

            def drain(ex_idx):
                if gens[0] is not None:
                    try:
                        next(gens[0])
                        return
                    except StopIteration:
                        gens[0] = None
                if ex_idx >= 2 and gens[1] is not None:
                    try:
                        next(gens[1])
                        return
                    except StopIteration:
                        gens[1] = None
            wait("t", s_ve, comb_ready)
            r2 = hn_r2
            for ex in range(NE + 1):
                shared = (ex == NE)
                for half in range(2):
                    m1_toks = []
                    for j in range(4):
                        src_t = d_sfc if shared else d_efc[ex]
                        m1_toks.append(ring_load([
                            lambda sl, j=j, src_t=src_t: (
                                sl[:, 0:4096].rearrange("p (k c) -> p k c", k=KD),
                                src_t.rearrange("(k p) m -> p k m", p=128)[
                                    :, :, half * 2048 + j * 512:half * 2048 + (j + 1) * 512]),
                        ], 4096))
                    r2_ms = [0] * 16
                    for j in range(4):
                        tok_m = m1_toks[j]
                        slot = tok_m[2]
                        ring_wait(tok_m)
                        for mc in range(4):
                            mo = j * 4 + mc
                            bk = mo % 4
                            pe_acquire(bk)
                            for k in range(KD):
                                last = (k == KD - 1)
                                inc = None
                                if last:
                                    inc = ring_free_inc(tok_m) if mc == 3 else pe_inc()
                                mm(pb[bk][:, :], slot[:, k * 512 + mc * 128:k * 512 + (mc + 1) * 128],
                                   xn[:, k * PT:(k + 1) * PT], start=(k == 0), stop=last, inc_sem=inc)
                            if mc == 3:
                                wait("v", s_wf[tok_m[0]], s_wf[tok_m[0]].n)
                            else:
                                wait("v", s_pe, s_pe.n)
                            sp = mo % 2
                            if mo >= 2:
                                wait("v", s_se, r2_ms[mo - 2])
                            m_rl = vop("tensor_scalar", rt_[sp][:, :], pb[bk][:, :], 0.0, None,
                                       ALU.max, inc=True)
                            set_guard(bk, s_ve)
                            wait("s", s_ve, m_rl)
                            r2_ms[mo] = sop("square", r2[:, mo * PT:(mo + 1) * PT],
                                            rt_[sp][:, :], inc=True)
                            drain(ex)
                    mlp1_done = s_se.n
                    src_t = d_spj if shared else d_epj[ex]
                    p_toks = []
                    for n in range(2):
                        for hk in range(2):
                            p_toks.append(ring_load([
                                lambda sl, n=n, hk=hk, src_t=src_t, half=half: (
                                    sl[:, 0:4096].rearrange("p (k c) -> p k c", k=KI // 2),
                                    src_t.rearrange("(k p) m -> p k m", p=128)[
                                        :, half * KI + hk * (KI // 2):half * KI + (hk + 1) * (KI // 2),
                                        n * 512:(n + 1) * 512]),
                            ], 4096))
                    for tp in p_toks:
                        ring_wait(tp)
                    wait("t", s_se, r2_ms[11])
                    for mt in range(TCH):
                        bks = (2, 3) if mt % 2 == 0 else (4, 5)
                        pe_acquire(bks[0])
                        pe_acquire(bks[1])
                        for k in range(KI):
                            if mt == 0 and 12 <= k:
                                wait("t", s_se, r2_ms[k])
                            last = (k == KI - 1)
                            lhs = r2[:, k * PT + mt * 128:k * PT + (mt + 1) * 128]
                            for n in range(2):
                                tok = p_toks[n * 2 + (k // 8)]
                                inc = None
                                if mt == TCH - 1 and (k % 8) == 7:
                                    inc = ring_free_inc(tok)
                                elif last:
                                    inc = pe_inc()
                                mm(pb[bks[n]][:, :], lhs, tok[2][:, (k % 8) * 512:(k % 8 + 1) * 512],
                                   start=(k == 0), stop=last, inc_sem=inc)
                        if mt == TCH - 1:
                            wait("v", s_wf[p_toks[3][0]], s_wf[p_toks[3][0]].n)
                            wait("v", s_wf[p_toks[1][0]], s_wf[p_toks[1][0]].n)
                        else:
                            wait("v", s_pe, s_pe.n)
                        for n in range(2):
                            acc_v = xt[:, mt * D + n * 512:mt * D + (n + 1) * 512]
                            if shared:
                                vop("tensor_tensor", acc_v, pb[bks[n]][:, :], acc_v, ALU.add, inc=True)
                            else:
                                vop("scalar_tensor_tensor", acc_v, pb[bks[n]][:, :],
                                    comb[:, mt * 8 + ex:mt * 8 + ex + 1], acc_v,
                                    ALU.mult, ALU.add, inc=True)
                            set_guard(bks[n], s_ve)
                        drain(ex)
            xn_free_pe = s_pe.n

            while gens[0] is not None or gens[1] is not None:
                drain(99)
            hn_done = cst["hn_done"]

            hg_last_tile = s_ve.n

            if ti + 1 < n_tiles:
                emit_xload(ti + 1)
                ams_next = {}
                gA_hold["gen"] = plan_A(ti + 1, ams_next)
                gA_hold["ams"] = ams_next
            fc_toks = []
            for j in range(4):
                fc_toks.append(ring_load([
                    lambda sl, j=j: (sl[:, 0:4096].rearrange("p (k c) -> p k c", k=KI),
                                     d_fc.rearrange("(k p) m -> p k m", p=128)[:, :, j * 256:(j + 1) * 256]),
                ], 4096))
            wait("t", s_ve, hn_done)
            for j in range(4):
                tok_fc = fc_toks[j]
                slot = tok_fc[2]
                ring_wait(tok_fc)
                for mc in range(2):
                    mg = j * 2 + mc
                    bk = mg % 2
                    pe_acquire(bk)
                    for k in range(KI):
                        last = (k == KI - 1)
                        inc = None
                        if last:
                            inc = ring_free_inc(tok_fc) if mc == 1 else pe_inc()
                        mm(pb[bk][:, :], slot[:, k * 256 + mc * 128:k * 256 + (mc + 1) * 128],
                           h2[:, k * PT:(k + 1) * PT], start=(k == 0), stop=last, inc_sem=inc)
                    if mc == 1:
                        wait("s", s_wf[tok_fc[0]], s_wf[tok_fc[0]].n)
                    else:
                        wait("s", s_pe, s_pe.n)
                    sop("activation", s_sb[:, mg * PT:(mg + 1) * PT], pb[bk][:, :],
                        AF.Silu, bias=v_fb[:, mg:mg + 1], inc=True)
                    set_guard(bk, s_se)
                    drain_A()
            fc_done = s_se.n

            cp_toks = []
            for j in range(4):
                cp_toks.append(ring_load([
                    lambda sl, j=j: (sl[:, 0:4096].rearrange("p (k c) -> p k c", k=KD),
                                     d_cp.rearrange("(k p) m -> p k m", p=128)[:, :, j * 512:(j + 1) * 512]),
                ], 4096))
            wait("t", s_se, fc_done)
            for j in range(4):
                tok_cp = cp_toks[j]
                slot = tok_cp[2]
                ring_wait(tok_cp)
                for mc in range(4):
                    c = j * 4 + mc
                    bk = 2 + c % 2
                    pe_acquire(bk)
                    for k in range(KD):
                        last = (k == KD - 1)
                        inc = None
                        if last:
                            inc = ring_free_inc(tok_cp) if mc == 3 else pe_inc()
                        mm(pb[bk][:, :], slot[:, k * 512 + mc * 128:k * 512 + (mc + 1) * 128],
                           s_sb[:, k * PT:(k + 1) * PT], start=(k == 0), stop=last, inc_sem=inc)
                    if mc == 3:
                        wait("v", s_wf[tok_cp[0]], s_wf[tok_cp[0]].n)
                    else:
                        wait("v", s_pe, s_pe.n)
                    vop("scalar_tensor_tensor", h2[:, c * PT:(c + 1) * PT], pb[bk][:, :],
                        v_pb[:, c:c + 1], hg[:, c * PT:(c + 1) * PT], ALU.add, ALU.add, inc=True)
                    set_guard(bk, s_ve)
                    drain_A()
            h2_done = s_ve.n

            out_toks = []
            for n in range(2):
                for hk in range(2):
                    out_toks.append(ring_load([
                        lambda sl, n=n, hk=hk: (
                            sl[:, 0:4096].rearrange("p (k c) -> p k c", k=KI // 2),
                            d_out.rearrange("(k p) m -> p k m", p=128)[
                                :, hk * (KI // 2):(hk + 1) * (KI // 2), n * 512:(n + 1) * 512]),
                    ], 4096))
            wait("t", s_ve, h2_done)
            for tp in out_toks:
                ring_wait(tp)
            for mt in range(TCH):
                bks = (2, 3) if mt % 2 == 0 else (4, 5)
                pe_acquire(bks[0])
                pe_acquire(bks[1])
                for k in range(KI):
                    last = (k == KI - 1)
                    lhs = h2[:, k * PT + mt * 128:k * PT + (mt + 1) * 128]
                    for n in range(2):
                        tok = out_toks[n * 2 + (k // 8)]
                        inc = None
                        if mt == TCH - 1 and (k % 8) == 7:
                            inc = ring_free_inc(tok)
                        elif last:
                            inc = pe_inc()
                        mm(pb[bks[n]][:, :], lhs, tok[2][:, (k % 8) * 512:(k % 8 + 1) * 512],
                           start=(k == 0), stop=last, inc_sem=inc)
                if mt == TCH - 1:
                    wait("v", s_wf[out_toks[3][0]], s_wf[out_toks[3][0]].n)
                    wait("v", s_wf[out_toks[1][0]], s_wf[out_toks[1][0]].n)
                else:
                    wait("v", s_pe, s_pe.n)
                for n in range(2):
                    acc_v = xt[:, mt * D + n * 512:mt * D + (n + 1) * 512]
                    vop("tensor_tensor", acc_v, pb[bks[n]][:, :], acc_v, ALU.add, inc=True)
                    set_guard(bks[n], s_ve)
                drain_A()

            acc_done_prev = s_ve.n

            while gA_hold["gen"] is not None:
                drain_A()

            wait("y", s_ve, acc_done_prev)
            em("y", lambda y, a=xt, dst=d_o[ti * PT:(ti + 1) * PT, :]:
               y.dma_start(dst.rearrange("(t p) f -> p t f", p=128),
                           a[:, :].rearrange("p (t f) -> p t f", t=TCH)).then_inc(s_st.h, 16))
            s_st.n += 16
            store_done[ti] = s_st.n

        wait("y", s_st, s_st.n)

        @block.tensor
        def _(t):
            for f in P["t"]:
                f(t)

        @block.vector
        def _(v):
            for f in P["v"]:
                f(v)

        @block.scalar
        def _(s_):
            for f in P["s"]:
                f(s_)

        @block.gpsimd
        def _(g):
            for f in P["g"]:
                f(g)

        @block.sync
        def _(y):
            for f in P["y"]:
                f(y)

    return nc


def _prep_weights(inp):
    f = np.ascontiguousarray
    w = {}
    w["w_ip"] = f(inp["L0_attn_in_proj_w"].T)
    cw = inp["L0_attn_conv_w"].reshape(8, 2, 128, 2, 128, 4)
    w["w_conv"] = f(cw.transpose(5, 4, 3, 0, 1, 2).reshape(4, 128, 4096))
    w["w_fc"] = f(inp["L0_attn_fc_w"].T)
    w["w_cproj"] = f(inp["L0_attn_cproj_w"].T)
    w["w_out"] = f(inp["L0_attn_out_w"].T)
    w["w_router"] = f(inp["L0_ffn_router_w"].T)
    w["w_efc"] = f(inp["L0_ffn_w_fc"].transpose(0, 2, 1))
    w["w_eproj"] = f(inp["L0_ffn_w_proj"].transpose(0, 2, 1))
    w["w_sfc"] = f(inp["L0_ffn_shared_fc"].T)
    w["w_sproj"] = f(inp["L0_ffn_shared_proj"].T)
    w["b_conv"] = f(inp["L0_attn_conv_b"].reshape(16, 128).T)
    w["b_lng"] = f(inp["L0_attn_ln_g"].reshape(16, 128).T)
    w["b_lnb"] = f(inp["L0_attn_ln_b"].reshape(16, 128).T)
    w["b_fc"] = f(inp["L0_attn_fc_b"].reshape(8, 128).T)
    w["b_cproj"] = f(inp["L0_attn_cproj_b"].reshape(16, 128).T)
    w["ident"] = np.eye(128, dtype=np.float32)
    return w


def kernel(**inputs):
    x_full = np.asarray(inputs["L0_attn_hidden_states"], dtype=np.float32)
    w = _prep_weights({k: np.asarray(v, dtype=np.float32) for k, v in inputs.items()
                       if k != "L0_attn_hidden_states"})
    nc = build_nc(16)
    in_maps = []
    for c in range(NCORES):
        m = dict(w)
        m["x"] = np.ascontiguousarray(x_full[c * BPC:(c + 1) * BPC].reshape(TOK, D))
        in_maps.append(m)
    res = run_bass_kernel_spmd(nc, in_maps, core_ids=list(range(NCORES)))
    outs = [r["out"].reshape(BPC, SEQ, D) for r in res.results]
    return np.concatenate(outs, axis=0)
